# version 01 (final)
# baseline (speedup 1.0000x reference)
import math
from contextlib import ExitStack

import numpy as np
import ml_dtypes

import concourse.bass as bass
import concourse.mybir as mybir
from concourse.bass_utils import run_bass_kernel_spmd

F32 = mybir.dt.float32
BF16 = mybir.dt.bfloat16
AF = mybir.ActivationFunctionType
ALU = mybir.AluOpType
AXX = mybir.AxisListType.X

P = 128
DM = 2048
NKC = 16
NTT = 8
NH = 16
HD = 128
EPS = 1e-5
MSHIFT = 30.0
RSQ = 1.0 / math.sqrt(HD)

_SLOPES = np.array([2.0 ** (-(i + 1) / 2.0) for i in range(NH)])
_RH = np.exp(-_SLOPES)
_RC = np.exp(-_SLOPES * 128.0)


class Sem:
    def __init__(self, ctx, nc, name):
        self.h = ctx.enter_context(nc.semaphore(name))
        self.n = 0

    def bump(self, k=1):
        self.n += k
        return self.n


class EQ:
    def __init__(self, name):
        self.name = name
        self.fs = []


class MK:
    def __init__(self, nc, ctx, upto, dbg=(), aa_heads=NH, use_cc=True, ln_only=False, aa_sub=9, no_rot=False):
        self.nc = nc
        self.ctx = ctx
        self.upto = upto
        self.dbg = list(dbg)
        self.dbg_map = {}
        self.aa_heads = aa_heads
        self.use_cc = use_cc
        self.ln_only = ln_only
        self.aa_sub = aa_sub
        self.no_rot = no_rot

    def dump(self, tag, ap, wait=()):
        if tag not in self.dbg:
            return
        idx = self.dbg.index(tag)
        if not isinstance(ap, bass.AP):
            ap = ap[:, :]
        out = self.dbg_out[idx]
        np_, nf = ap.shape[0], ap.shape[-1]
        v = self.emit(self.GQ, lambda e, ap=ap, out=out, np_=np_, nf=nf: e.dma_start(
            out=out[0:np_, 0:nf], in_=ap),
            wait=wait, inc=self.d_w, inc_k=16)
        for q in (self.TQ, self.VQ, self.AQ, self.YQ):
            q.fs.append(lambda eng, v=v: eng.wait_ge(self.d_w.h, v))

    last_rot_po = 0

    def vf(self):
        return [(self.s_ve, self.s_ve.n)]

    def emit(self, q, f, wait=(), inc=None, inc_k=1):
        v = inc.bump(inc_k) if inc is not None else None
        wl = tuple((s.h, val) for (s, val) in wait if val > 0)

        def g(eng, f=f, wl=wl, inc=inc, inc_k=inc_k):
            for h, val in wl:
                eng.wait_ge(h, val)
            r = f(eng)
            if inc is not None:
                r.then_inc(inc.h, inc_k)
        q.fs.append(g)
        return v

    def declare(self):
        nc = self.nc
        ctx = self.ctx

        def din(name, shape, dtype=F32):
            return nc.dram_tensor(name, list(shape), dtype, kind="ExternalInput")

        self.x_in = din("x_in", [P, NTT * DM])
        self.wqa = din("wqa", [NH, P, DM])
        self.wka = din("wka", [NH, P, DM])
        self.wva = din("wva", [4, NKC, P, 512])
        self.woa = din("woa", [4, NKC, P, 512])
        self.rmsw_d = din("rmsw", [P, NH])
        self.gatew_d = din("gatew", [P, P])
        self.wg_d = din("wg", [8, 8, P, DM])
        self.wu_d = din("wu", [8, 8, P, DM])
        self.wd_d = din("wd", [8, 4, 8, P, 512])
        self.sg_d = din("sg", [NKC, P, DM])
        self.su_d = din("su", [NKC, P, DM])
        self.sd_d = din("sd", [4, NKC, P, 512])
        self.wqb = din("wqb", [NH, P, DM])
        self.wkb = din("wkb", [NH, P, DM])
        self.wvb = din("wvb", [4, NKC, P, 512])
        self.bqb_d = din("bqb", [P, NH])
        self.bkb_d = din("bkb", [P, NH])
        self.bvb_d = din("bvb", [1, DM])
        self.identf_d = din("identf", [P, P])
        self.cosc_d = din("cos_c", [32, 1024], BF16)
        self.sinc_d = din("sin_c", [32, 1024], BF16)
        self.cost_d = din("cos_t", [P, 256], BF16)
        self.sint_d = din("sin_t", [P, 256], BF16)
        self.dmaskt_d = din("dmaskt", [P, DM], BF16)
        self.qdect_d = din("qdect", [P, DM], BF16)
        self.kdecc_d = din("kdecc", [P, NH])
        self.dwcol_d = din("dwcol", [P, 64])
        self.constf_d = din("constf", [P, 8])

        self.x_out = nc.dram_tensor("x_out", [P, NTT * DM], F32, kind="ExternalOutput")
        self.dbg_out = [nc.dram_tensor(f"dbg{i}", [P, NKC * 1024], F32, kind="ExternalOutput")
                        for i in range(len(self.dbg))]

        self.agd_in = [nc.dram_tensor(f"agdi{h}", [P, HD], BF16, kind="Internal") for h in range(NH)]
        self.agd_out = [nc.dram_tensor(f"agdo{h}", [4 * P, HD], BF16, kind="Internal") for h in range(NH)]
        self.agk_in = nc.dram_tensor("agki", [DM, 1024], BF16, kind="Internal")
        self.agk_out = nc.dram_tensor("agko", [4 * DM, 1024], BF16, kind="Internal")
        self.agv_in = nc.dram_tensor("agvi", [1024, DM], BF16, kind="Internal")
        self.agv_out = nc.dram_tensor("agvo", [4 * 1024, DM], BF16, kind="Internal")
        self.RG = [[0, 1, 2, 3], [4, 5, 6, 7]]

        def sbt(name, shape, dt):
            return ctx.enter_context(nc.sbuf_tensor(name, list(shape), dt))

        self.x_sb = sbt("x_sb", [P, NTT * DM], F32)
        self.xlnT = sbt("xlnT", [P, NKC * 1024], BF16)
        self.bB = sbt("bf16B", [P, 16384], BF16)
        self.bC = sbt("bf16C", [P, 24576], BF16)
        self.sq = sbt("sq", [P, 1024], F32)
        self.sc = sbt("scc", [P, 16], F32)
        self.srow = sbt("srow", [1, P], F32)
        self.onesb = sbt("onesb", [1, P], BF16)
        self.onescol = sbt("onescol", [P, 1], BF16)
        self.c_id = sbt("c_id", [P, P], F32)
        self.c_cos = sbt("c_cos", [32, 1024], BF16)
        self.c_sin = sbt("c_sin", [32, 1024], BF16)
        self.c_cost = sbt("c_cost", [P, 256], BF16)
        self.c_sint = sbt("c_sint", [P, 256], BF16)
        self.c_dmask = sbt("c_dmask", [P, DM], BF16)
        self.c_qdec = sbt("c_qdec", [P, DM], BF16)
        self.c_kdec = sbt("c_kdec", [P, NH], F32)
        self.c_dw = sbt("c_dw", [P, 64], F32)
        self.c_rmsw = sbt("c_rmsw", [P, NH], F32)
        self.c_gate = sbt("c_gate", [P, P], BF16)
        self.c_bq = sbt("c_bq", [P, NH], F32)
        self.c_bk = sbt("c_bk", [P, NH], F32)
        self.c_cf = sbt("c_cf", [P, 8], F32)
        nc.const_aps.register(F32, float(EPS), self.c_cf[:, 0:1])
        nc.const_aps.register(F32, -0.5, self.c_cf[:, 1:2])
        nc.const_aps.register(F32, float(RSQ), self.c_cf[:, 2:3])
        nc.const_aps.register(F32, float(-MSHIFT), self.c_cf[:, 3:4])
        self.bvrow_bf = sbt("bvrow_bf", [1, DM], BF16)
        self.srow_bf = sbt("srow_bf", [1, P], BF16)
        self.rsA = sbt("rsA", [P, 1024], BF16)
        self.rsB = sbt("rsB", [P, 1024], BF16)
        self.s_st = sbt("s_st", [P, HD], F32)
        self.s_bf = sbt("s_bf", [P, HD], BF16)
        self.d_sb = sbt("d_sb", [P, HD], F32)
        self.gd_sb = sbt("gd_sb", [P, 512], BF16)
        self.cw_sb = sbt("cw_sb", [P, 64], F32)
        self.cwsc = sbt("cwsc", [P, 8], F32)
        self.cwm = sbt("cwm", [P, 8], F32)
        self.ln3a = sbt("ln3a", [P, NTT], F32)
        self.ln3b = sbt("ln3b", [P, NTT], F32)
        self.u_sb = self.sq[:, 512:1024].bitcast(BF16)
        self.q_bf = self.bC[:, 16384:17408]
        self.k_bf = self.bC[:, 17408:18432]
        self.ktm = self.bC[:, 18432:19456]
        self.v_hg = self.bC[:, 19456:23552]
        self.stm = self.bC[:, 23552:23808]
        self.qdq = self.bC[:, 23808:24064]
        self.bv_bc = self.bC[:, 6144:8192]

        self.psA = ctx.enter_context(nc.psum_tensor("psA", [P, 1024], F32))
        self.psB = ctx.enter_context(nc.psum_tensor("psB", [P, 1024], F32))
        self.psC = ctx.enter_context(nc.psum_tensor("psC", [P, 1024], F32))
        self.psT = ctx.enter_context(nc.psum_tensor("psT", [P, 1024], F32))

        self.s_pe = Sem(ctx, nc, "s_pe")
        self.s_ve = Sem(ctx, nc, "s_ve")
        self.s_ac = Sem(ctx, nc, "s_ac")
        self.s_po = Sem(ctx, nc, "s_po")
        self.d_w = Sem(ctx, nc, "d_w")
        self.d_x = Sem(ctx, nc, "d_x")
        self.d_kv = Sem(ctx, nc, "d_kv")
        self.d_sc = Sem(ctx, nc, "d_sc")
        self.d_kb = Sem(ctx, nc, "d_kb")
        self.d_st = Sem(ctx, nc, "d_st")
        self.ccd = [Sem(ctx, nc, f"ccd{h}") for h in range(NH)]
        self.cck = Sem(ctx, nc, "cck")
        self.ccv = Sem(ctx, nc, "ccv")

        self.TQ = EQ("tensor")
        self.VQ = EQ("vector")
        self.AQ = EQ("scalar")
        self.GQ = EQ("gpsimd")
        self.YQ = EQ("sync")

        self.X_DEP = []
        self.psT_ev = [0, 0]
        self.xlnT_dep = []

    def initial_loads(self):
        E = self.emit
        E(self.YQ, lambda e: e.dma_start(out=self.x_sb[:, :], in_=self.x_in[:, :]),
          inc=self.d_x, inc_k=16)
        self.X_DEP = [(self.d_x, 16)]
        pairs = [(self.c_id, self.identf_d), (self.c_cos, self.cosc_d), (self.c_sin, self.sinc_d),
                 (self.c_cost, self.cost_d), (self.c_sint, self.sint_d), (self.c_dmask, self.dmaskt_d),
                 (self.c_qdec, self.qdect_d), (self.c_kdec, self.kdecc_d), (self.c_dw, self.dwcol_d),
                 (self.c_rmsw, self.rmsw_d), (self.c_bq, self.bqb_d), (self.c_bk, self.bkb_d),
                 (self.bvrow_bf, self.bvb_d), (self.c_gate, self.gatew_d),
                 (self.c_cf, self.constf_d)]
        for t, s in pairs:
            E(self.GQ, lambda e, t=t, s=s: e.dma_start(out=t[:, :], in_=s[:, :]),
              inc=self.d_w, inc_k=16)
        self.CONSTS = (self.d_w, self.d_w.n)
        E(self.VQ, lambda e: e.memset(self.onesb[:, :], 1.0), inc=self.s_ve)
        E(self.VQ, lambda e: e.memset(self.onescol[:, :], 1.0), inc=self.s_ve)
        cw = self.CONSTS
        for q in (self.TQ, self.VQ, self.AQ, self.GQ):
            q.fs.append(lambda eng, cw=cw: eng.wait_ge(cw[0].h, cw[1]))

    def ln(self, keep_stats=False):
        E = self.emit
        sc, sq, x_sb = self.sc, self.sq, self.x_sb
        s_pe, s_ve, s_ac = self.s_pe, self.s_ve, self.s_ac
        sq_free = 0
        sq_dve_free = [0]
        dview = self.xlnT.ap().rearrange("p (f c) -> p f c", c=1024)
        for t in range(NTT):
            xt = x_sb[:, t * DM:(t + 1) * DM]
            E(self.VQ, lambda e, xt=xt: e.reduce_sum(sc[:, 0:1], xt, axis=AXX),
              wait=self.X_DEP + [(self.d_w, self.CONSTS[1])], inc=s_ve)
            pa = E(self.AQ, lambda e, xt=xt: e.activation(sq[:, 0:1024], xt[:, 0:1024], AF.Square),
                   wait=self.X_DEP + [(s_pe, sq_free), (s_ve, sq_dve_free[0])], inc=s_ac)
            E(self.VQ, lambda e: e.reduce_sum(sc[:, 1:2], sq[:, 0:1024], axis=AXX),
              wait=[(s_ac, pa)], inc=s_ve)
            sq_dve_free[0] = self.s_ve.n
            pa = E(self.AQ, lambda e, xt=xt: e.activation(sq[:, 0:1024], xt[:, 1024:2048], AF.Square),
                   wait=[(s_ve, sq_dve_free[0])], inc=s_ac)
            E(self.VQ, lambda e: e.reduce_sum(sc[:, 2:3], sq[:, 0:1024], axis=AXX),
              wait=[(s_ac, pa)], inc=s_ve)
            sq_dve_free[0] = self.s_ve.n
            E(self.VQ, lambda e: e.tensor_scalar(sc[:, 3:4], sc[:, 0:1], 1.0 / DM, None, ALU.mult),
              wait=self.vf(), inc=s_ve)
            E(self.VQ, lambda e: e.tensor_tensor(sc[:, 4:5], sc[:, 1:2], sc[:, 2:3], ALU.add),
              wait=self.vf(), inc=s_ve)
            E(self.VQ, lambda e: e.tensor_scalar(sc[:, 4:5], sc[:, 4:5], 1.0 / DM, None, ALU.mult),
              wait=self.vf(), inc=s_ve)
            E(self.VQ, lambda e: e.tensor_tensor(sc[:, 5:6], sc[:, 3:4], sc[:, 3:4], ALU.mult),
              wait=self.vf(), inc=s_ve)
            pv = E(self.VQ, lambda e: e.tensor_tensor(sc[:, 4:5], sc[:, 4:5], sc[:, 5:6], ALU.subtract),
                   wait=self.vf(), inc=s_ve)
            E(self.AQ, lambda e: e.activation(sc[:, 6:7], sc[:, 4:5], AF.Ln, bias=float(EPS)),
              wait=[(s_ve, pv)], inc=s_ac)
            pa = E(self.AQ, lambda e: e.activation(sc[:, 6:7], sc[:, 6:7], AF.Exp, scale=-0.5), inc=s_ac)
            pv = E(self.VQ, lambda e: e.tensor_scalar(sc[:, 7:8], sc[:, 3:4], sc[:, 6:7], -1.0,
                                                      ALU.mult, ALU.mult),
                   wait=[(s_ac, pa)] + self.vf(), inc=s_ve)
            if keep_stats:
                E(self.VQ, lambda e, t=t: e.tensor_scalar(self.ln3a[:, t:t + 1], sc[:, 6:7], 1.0,
                                                          None, ALU.add), inc=s_ve)
                E(self.VQ, lambda e, t=t: e.tensor_copy(self.ln3b[:, t:t + 1], sc[:, 7:8]),
                  wait=self.vf(), inc=s_ve)
            if t == 0:
                self.dump("lnsc", sc[:, 0:16], wait=[(s_ve, pv)])
            for half in range(2):
                pa2 = E(self.AQ, lambda e, xt=xt, half=half: e.activation(
                    sq[:, 0:1024], xt[:, half * 1024:(half + 1) * 1024], AF.Identity,
                    bias=sc[:, 7:8], scale=sc[:, 6:7]),
                    wait=[(s_ve, pv), (s_pe, sq_free), (s_ve, sq_dve_free[0])], inc=s_ac)
                if half == 0 and t == 0:
                    self.dump("lnsq", sq[:, 0:1024], wait=[(s_ac, pa2)])
                for grp in range(2):
                    for j in range(4):
                        w = []
                        if j == 0:
                            w = [(s_ac, pa2), (s_ve, self.psT_ev[grp])]
                        E(self.TQ, lambda e, grp=grp, j=j: e.transpose(
                            self.psT[:, grp * 512 + j * P: grp * 512 + (j + 1) * P],
                            sq[:, (grp * 4 + j) * P:(grp * 4 + j + 1) * P], self.c_id[:, :]),
                            wait=w, inc=(s_pe if j == 3 else None))
                    ppe = s_pe.n
                    fc0 = half * 8 + grp * 4
                    self.psT_ev[grp] = E(self.VQ, lambda e, fc0=fc0, t=t, grp=grp: e.tensor_copy(
                        dview[:, fc0:fc0 + 4, t * P:(t + 1) * P],
                        self.psT[:, grp * 512:(grp + 1) * 512].rearrange("p (f c) -> p f c", c=P)),
                        wait=[(s_pe, ppe)], inc=s_ve)
                sq_free = s_pe.n
        self.xlnT_dep = [(s_ve, s_ve.n)]

    def rotary_fm(self, buf, pe_val, base_ac):
        E = self.emit
        rsA, rsB = self.rsA, self.rsB
        s_ve, s_po = self.s_ve, self.s_po
        if self.no_rot:
            return (0, self.last_rot_po)
        v0 = E(self.VQ, lambda e, buf=buf: e.tensor_copy(rsA[0:32, :], buf[64:96, :]),
               wait=[(self.s_ac, base_ac), (s_po, self.last_rot_po)], inc=s_ve)
        E(self.GQ, lambda e: e.tensor_tensor(rsB[0:32, :], rsA[0:32, :], self.c_sin[:, :], ALU.mult),
          wait=[(s_ve, v0)], inc=s_po)
        p2 = E(self.GQ, lambda e: e.tensor_tensor(rsB[32:64, :], rsA[0:32, :], self.c_cos[:, :], ALU.mult),
               inc=s_po)
        v1 = E(self.VQ, lambda e, buf=buf: e.tensor_tensor(rsA[0:32, :], buf[0:32, :], self.c_cos[:, :], ALU.mult),
               wait=[(s_po, p2)], inc=s_ve)
        v2 = E(self.VQ, lambda e, buf=buf: e.tensor_tensor(rsA[32:64, :], buf[0:32, :], self.c_sin[:, :], ALU.mult),
               inc=s_ve)
        E(self.GQ, lambda e, buf=buf: e.tensor_tensor(buf[0:32, :], rsA[0:32, :], rsB[0:32, :], ALU.add),
          wait=[(s_ve, v2)], inc=s_po)
        p4 = E(self.GQ, lambda e, buf=buf: e.tensor_tensor(buf[64:96, :], rsB[32:64, :], rsA[32:64, :],
                                                           ALU.subtract), inc=s_po)
        self.last_rot_po = p4
        return (v2, p4)

    def rotary_tm(self, buf, dep_ve):
        E = self.emit
        rsA, rsB = self.rsA, self.rsB
        s_ve, s_po = self.s_ve, self.s_po
        bv = buf.rearrange("p (t d) -> p t d", d=P)
        r2A = rsA.ap().rearrange("p (t d) -> p t d", d=32)
        r2B = rsB.ap().rearrange("p (t d) -> p t d", d=32)
        cv = self.c_cost.ap().rearrange("p (t d) -> p t d", d=32)
        sv = self.c_sint.ap().rearrange("p (t d) -> p t d", d=32)
        v0 = E(self.VQ, lambda e: e.tensor_copy(r2A[:, 0:8, :], bv[:, :, 64:96]),
               wait=[(s_ve, dep_ve), (s_po, self.last_rot_po)] + self.vf(), inc=s_ve)
        p1 = E(self.GQ, lambda e: e.tensor_tensor(r2B[:, 0:8, :], r2A[:, 0:8, :], sv[:, :, :], ALU.mult),
               wait=[(s_ve, v0)], inc=s_po)
        p2 = E(self.GQ, lambda e: e.tensor_tensor(r2B[:, 8:16, :], r2A[:, 0:8, :], cv[:, :, :], ALU.mult),
               inc=s_po)
        v1 = E(self.VQ, lambda e: e.tensor_tensor(r2A[:, 0:8, :], bv[:, :, 0:32], cv[:, :, :], ALU.mult),
               wait=[(s_po, p2)], inc=s_ve)
        v2 = E(self.VQ, lambda e: e.tensor_tensor(r2A[:, 8:16, :], bv[:, :, 0:32], sv[:, :, :], ALU.mult),
               inc=s_ve)
        v3 = E(self.VQ, lambda e: e.tensor_tensor(bv[:, :, 0:32], r2A[:, 0:8, :], r2B[:, 0:8, :], ALU.add),
               wait=[(s_ve, v2), (s_po, p1)] + self.vf(), inc=s_ve)
        p4 = E(self.GQ, lambda e: e.tensor_tensor(bv[:, :, 64:96], r2B[:, 8:16, :], r2A[:, 8:16, :],
                                                  ALU.subtract), wait=[(s_ve, v2)], inc=s_po)
        self.last_rot_po = p4
        return (v3, p4)

    def attn_a(self):
        E = self.emit
        nc = self.nc
        s_pe, s_ve, s_ac, s_po = self.s_pe, self.s_ve, self.s_ac, self.s_po
        d_w, d_kv, d_sc = self.d_w, self.d_kv, self.d_sc
        psA, psB, psC, psT = self.psA, self.psB, self.psC, self.psT
        xlnT, bC = self.xlnT, self.bC

        self.ln(keep_stats=False)
        self.dump("xlnT", xlnT[:, :], wait=self.xlnT_dep)
        if self.ln_only:
            return

        wk_slot = [bC[:, 0:2048], bC[:, 2048:4096]]
        wq_slot = [bC[:, 4096:6144], bC[:, 6144:8192]]
        wv_slot = [bC[:, 8192:16384]]
        wk_credit = [0, 0]
        wq_credit = [0, 0]
        wv_credit = [0]
        psA_ve = 0
        psA_ac = 0
        psB_ve = 0
        psC_ve = 0
        attn_evac = []

        def load_wk(h):
            return E(self.GQ, lambda e, h=h: e.dma_start(out=wk_slot[h % 2], in_=self.wka[h]),
                     wait=[(s_pe, wk_credit[h % 2])], inc=d_w, inc_k=16)

        def load_wq(h):
            return E(self.GQ, lambda e, h=h: e.dma_start(out=wq_slot[h % 2], in_=self.wqa[h]),
                     wait=[(s_pe, wq_credit[h % 2])], inc=d_w, inc_k=16)

        def load_wv(hg):
            sl = wv_slot[0].rearrange("p (k j) -> p k j", j=512)
            last = None
            for i in range(4):
                last = E(self.GQ, lambda e, hg=hg, i=i, sl=sl: e.dma_start(
                    out=sl[:, i * 4:(i + 1) * 4, :],
                    in_=self.wva[hg, i * 4:(i + 1) * 4].transpose([1, 0, 2])),
                    wait=([(s_pe, wv_credit[0])] if i == 0 else []), inc=d_w, inc_k=16)
            return last

        wk_ready = {0: load_wk(0)}
        wv_ready = {0: load_wv(0)}
        wq_ready = {0: load_wq(0)}
        if self.aa_sub <= -1:
            return
        vprev = [0] * NTT
        d_dma_prev = [0]
        gd_used = [0]

        for h in range(self.aa_heads):
            hg = h // 4
            hh = h % 4
            rc_h = float(np.float32(_RC[h]))
            if h + 1 < NH:
                wk_ready[h + 1] = load_wk(h + 1)
                if (h + 1) % 4 == 0:
                    wv_ready[hg + 1] = load_wv(hg + 1)
                wq_ready[h + 1] = load_wq(h + 1)

            wk_sl = wk_slot[h % 2]
            wq_sl = wq_slot[h % 2]
            wv_sl = wv_slot[0]

            for blk in range(2):
                for kc in range(NKC):
                    w = []
                    if blk == 0 and kc == 0:
                        w = [(d_w, wk_ready[h]), (s_ve, psA_ve), (s_ac, psA_ac)] + self.xlnT_dep
                    E(self.TQ, lambda e, blk=blk, kc=kc, wk_sl=wk_sl: e.matmul(
                        psA[:, blk * 512:(blk + 1) * 512],
                        wk_sl[:, kc * P:(kc + 1) * P],
                        xlnT[:, kc * 1024 + blk * 512: kc * 1024 + (blk + 1) * 512],
                        start=(kc == 0), stop=(kc == NKC - 1)),
                        wait=w, inc=(s_pe if kc == NKC - 1 else None))
            kproj = s_pe.n
            wk_credit[h % 2] = kproj
            if self.aa_sub <= 0:
                continue
            pa_k = E(self.AQ, lambda e: e.activation(self.k_bf[:, :], psA[:, :], AF.Identity),
                     wait=[(s_pe, kproj)], inc=s_ac)
            if self.aa_sub <= 0.5:
                continue
            pv_kf = E(self.VQ, lambda e: e.tensor_copy(self.sq[:, 0:1024], psA[:, :]),
                      wait=[(s_pe, kproj), (s_ac, pa_k)], inc=s_ve)
            if self.aa_sub <= 0.7:
                continue
            rot_k = self.rotary_fm(self.k_bf, kproj, pa_k)
            psA_ve = rot_k[0]
            psA_ac = pa_k
            if h == 0:
                self.dump("k0", self.k_bf, wait=[(s_ve, rot_k[0]), (s_po, rot_k[1]), (s_ac, pa_k)])
            if self.aa_sub <= 1:
                continue

            for grp in range(2):
                for j in range(4):
                    w = []
                    if j == 0:
                        w = [(s_ve, pv_kf), (s_ve, self.psT_ev[grp])]
                    E(self.TQ, lambda e, grp=grp, j=j: e.transpose(
                        psT[:, grp * 512 + j * P: grp * 512 + (j + 1) * P],
                        self.sq[:, (grp * 4 + j) * P:(grp * 4 + j + 1) * P], self.c_id[:, :]),
                        wait=w, inc=(s_pe if j == 3 else None))
            ktm_tp = s_pe.n
            ev = E(self.VQ, lambda e: e.tensor_copy(self.ktm[:, :], psT[:, :]),
                   wait=[(s_pe, ktm_tp)], inc=s_ve)
            self.psT_ev[0] = ev
            self.psT_ev[1] = ev
            ktm_rot = self.rotary_tm(self.ktm, ev)
            pv_kd = E(self.VQ, lambda e, h=h: e.tensor_scalar(
                self.ktm[:, :], self.ktm[:, :], self.c_kdec[:, h:h + 1], None, ALU.mult),
                wait=[(s_po, ktm_rot[1]), (s_ve, ktm_rot[0])] + self.vf(), inc=s_ve)
            if h == 0:
                self.dump("ktm0", self.ktm, wait=[(s_ve, pv_kd)])
            if self.aa_sub <= 2:
                continue

            if hh == 0:
                for tt in range(NTT):
                    for kc in range(NKC):
                        w = []
                        if tt == 0 and kc == 0:
                            w = [(d_w, wv_ready[hg]), (s_ve, psB_ve)]
                        elif kc == 0 and tt >= 2:
                            w = [(s_ve, vprev[tt - 2])]
                        E(self.TQ, lambda e, tt=tt, kc=kc, wv_sl=wv_sl: e.matmul(
                            psB[:, (tt % 2) * 512:((tt % 2) + 1) * 512],
                            xlnT[:, kc * 1024 + tt * P: kc * 1024 + (tt + 1) * P],
                            wv_sl[:, kc * 512:(kc + 1) * 512],
                            start=(kc == 0), stop=(kc == NKC - 1)),
                            wait=w, inc=(s_pe if kc == NKC - 1 else None))
                    vg = s_pe.n
                    vprev[tt] = E(self.VQ, lambda e, tt=tt: e.tensor_copy(
                        self.v_hg[:, tt * 512:(tt + 1) * 512],
                        psB[:, (tt % 2) * 512:((tt % 2) + 1) * 512]),
                        wait=[(s_pe, vg)], inc=s_ve)
                psB_ve = vprev[NTT - 1]
                wv_credit[0] = s_pe.n
                if h == 0:
                    self.dump("v0", self.v_hg, wait=[(s_ve, vprev[NTT - 1])])
            vdep = vprev[NTT - 1]
            if self.aa_sub <= 3:
                continue

            for t in range(NTT):
                w = []
                if t == 0:
                    w = [(s_ve, pv_kd), (s_ve, psC_ve), (s_ve, vdep)]
                E(self.TQ, lambda e, t=t, hh=hh: e.matmul(
                    psC[:, t * P:(t + 1) * P],
                    self.ktm[:, t * P:(t + 1) * P],
                    self.v_hg[:, t * 512 + hh * P: t * 512 + (hh + 1) * P],
                    start=True, stop=True),
                    wait=w, inc=(s_pe if t == NTT - 1 else None))
            umm = s_pe.n
            pv_u = E(self.VQ, lambda e: e.tensor_copy(self.u_sb[:, :], psC[:, :]),
                     wait=[(s_pe, umm)], inc=s_ve)
            psC_ve = pv_u
            E(self.VQ, lambda e: e.tensor_copy(self.d_sb[:, :], self.u_sb[:, 0:P]),
              wait=[(d_sc, d_dma_prev[0])] + self.vf(), inc=s_ve)
            for t in range(1, NTT):
                E(self.VQ, lambda e, t=t, rc_h=rc_h: e.scalar_tensor_tensor(
                    self.d_sb[:, :], self.d_sb[:, :], rc_h, self.u_sb[:, t * P:(t + 1) * P],
                    ALU.mult, ALU.add), wait=self.vf(), inc=s_ve)
            if self.use_cc:
                pv_d = self.s_ve.n
                E(self.GQ, lambda e, h=h: e.dma_start(out=self.agd_in[h][:, :], in_=self.d_sb[:, :]),
                  wait=[(s_ve, pv_d)], inc=d_sc, inc_k=16)
                dscv = d_sc.n
                d_dma_prev[0] = dscv
                E(self.GQ, lambda e, h=h: e.collective_compute(
                    "AllGather", ALU.bypass, replica_groups=self.RG,
                    ins=[self.agd_in[h].ap().opt()], outs=[self.agd_out[h].ap().opt()]),
                    wait=[(d_sc, dscv)], inc=self.ccd[h])

            for blk in range(2):
                for kc in range(NKC):
                    w = []
                    if blk == 0 and kc == 0:
                        w = [(d_w, wq_ready[h]), (s_ve, psA_ve), (s_ac, psA_ac), (s_po, rot_k[1])]
                    E(self.TQ, lambda e, blk=blk, kc=kc, wq_sl=wq_sl: e.matmul(
                        psA[:, blk * 512:(blk + 1) * 512],
                        wq_sl[:, kc * P:(kc + 1) * P],
                        xlnT[:, kc * 1024 + blk * 512: kc * 1024 + (blk + 1) * 512],
                        start=(kc == 0), stop=(kc == NKC - 1)),
                        wait=w, inc=(s_pe if kc == NKC - 1 else None))
            qproj = s_pe.n
            wq_credit[h % 2] = qproj
            pa_q = E(self.AQ, lambda e: e.activation(self.q_bf[:, :], psA[:, :], AF.Identity),
                     wait=[(s_pe, qproj)], inc=s_ac)
            rot_q = self.rotary_fm(self.q_bf, qproj, pa_q)
            psA_ve = rot_q[0]
            psA_ac = pa_q
            if h == 0:
                self.dump("q0", self.q_bf, wait=[(s_ve, rot_q[0]), (s_po, rot_q[1]), (s_ac, pa_q)])
            if self.aa_sub <= 4:
                continue

            if self.use_cc:
                E(self.YQ, lambda e, h=h: e.dma_start(
                    out=self.gd_sb.ap().rearrange("p (j e) -> p j e", e=HD),
                    in_=self.agd_out[h].ap().rearrange("(j p) e -> p j e", p=P)),
                    wait=[(self.ccd[h], 1), (s_ve, gd_used[0])], inc=d_kv, inc_k=16)
                gdv = d_kv.n
                E(self.VQ, lambda e, h=h: e.tensor_scalar(
                    self.s_st[:, :], self.gd_sb[:, 0:P], self.c_dw[:, h:h + 1], None, ALU.mult),
                    wait=[(d_kv, gdv)], inc=s_ve)
                for j in range(1, 4):
                    E(self.VQ, lambda e, j=j, h=h: e.scalar_tensor_tensor(
                        self.s_st[:, :], self.gd_sb[:, j * P:(j + 1) * P],
                        self.c_dw[:, j * NH + h:j * NH + h + 1],
                        self.s_st[:, :], ALU.mult, ALU.add), wait=self.vf(), inc=s_ve)
                gd_used[0] = s_ve.n
            else:
                E(self.VQ, lambda e: e.memset(self.s_st[:, :], 0.0), inc=s_ve)
            pv_s = E(self.VQ, lambda e: e.tensor_copy(self.s_bf[:, :], self.s_st[:, :]),
                     wait=self.vf(), inc=s_ve)
            if h == 0:
                self.dump("u0", self.u_sb, wait=[(s_ve, pv_u)])
                self.dump("s0", self.s_bf, wait=[(s_ve, pv_s)])
            if self.aa_sub <= 5:
                continue

            stm_ev = [0, 0]
            sup_v = pv_s
            last_inter = 0
            for t in range(NTT):
                sl = t % 2
                w = [(s_ve, rot_q[0]), (s_po, rot_q[1]), (s_ac, pa_q)]
                w.append((s_ve, stm_ev[sl]) if t >= 2 else (s_ve, self.psT_ev[sl]))
                E(self.TQ, lambda e, t=t, sl=sl: e.matmul(
                    psT[:, sl * 512: sl * 512 + P],
                    self.k_bf[:, t * P:(t + 1) * P],
                    self.q_bf[:, t * P:(t + 1) * P], start=True, stop=True),
                    wait=w, inc=s_pe)
                sTv = s_pe.n
                stm_ev[sl] = E(self.VQ, lambda e, sl=sl, h=h: e.tensor_tensor(
                    self.stm[:, sl * P:(sl + 1) * P], psT[:, sl * 512: sl * 512 + P],
                    self.c_dmask[:, h * P:(h + 1) * P], ALU.mult),
                    wait=[(s_pe, sTv)], inc=s_ve)
                pv_qdq = E(self.VQ, lambda e, t=t, sl=sl, h=h: e.tensor_tensor(
                    self.qdq[:, sl * P:(sl + 1) * P], self.q_bf[:, t * P:(t + 1) * P],
                    self.c_qdec[:, h * P:(h + 1) * P], ALU.mult), inc=s_ve)
                E(self.TQ, lambda e, t=t, hh=hh, sl=sl: e.matmul(
                    psA[:, t * P:(t + 1) * P],
                    self.v_hg[:, t * 512 + hh * P: t * 512 + (hh + 1) * P],
                    self.stm[:, sl * P:(sl + 1) * P], start=True, stop=False),
                    wait=[(s_ve, stm_ev[sl]), (s_ve, sup_v)])
                E(self.TQ, lambda e, t=t, sl=sl: e.matmul(
                    psA[:, t * P:(t + 1) * P], self.s_bf[:, :], self.qdq[:, sl * P:(sl + 1) * P],
                    start=False, stop=True),
                    wait=[(s_ve, pv_qdq)], inc=s_pe)
                last_inter = s_pe.n
                E(self.VQ, lambda e, t=t, rc_h=rc_h: e.scalar_tensor_tensor(
                    self.s_st[:, :], self.s_st[:, :], rc_h, self.u_sb[:, t * P:(t + 1) * P],
                    ALU.mult, ALU.add), wait=[(s_pe, last_inter)] + self.vf(), inc=s_ve)
                sup_v = E(self.VQ, lambda e: e.tensor_copy(self.s_bf[:, :], self.s_st[:, :]),
                          wait=self.vf(), inc=s_ve)
            self.psT_ev[0] = stm_ev[0]
            self.psT_ev[1] = stm_ev[1]
            attn_evac.append(E(self.VQ, lambda e, h=h: e.tensor_copy(
                self.bB[:, h * 1024:(h + 1) * 1024], psA[:, :]),
                wait=[(s_pe, last_inter)], inc=s_ve))
            psA_ve = attn_evac[-1]
            psA_ac = 0

        if self.aa_heads < NH:
            return
        self.dump("attno", self.bB[:, :], wait=[(self.s_ve, attn_evac[-1])])
        bview = self.bB.ap().rearrange("p (hh t c) -> p hh t c", t=NTT, c=P)
        sqvA = self.rsA.ap().rearrange("p (hh c) -> p hh c", c=P)
        sqvB = self.rsB.ap().rearrange("p (hh c) -> p hh c", c=P)
        all_heads = attn_evac[-1]
        wo_slot = [xlnT[:, 0:8192], xlnT[:, 8192:16384]]
        wo_credit = [s_pe.n, s_pe.n]

        def load_wo(fb):
            sl = wo_slot[fb % 2].rearrange("p (k j) -> p k j", j=512)
            last = None
            for i in range(4):
                last = E(self.GQ, lambda e, fb=fb, i=i, sl=sl: e.dma_start(
                    out=sl[:, i * 4:(i + 1) * 4, :],
                    in_=self.woa[fb, i * 4:(i + 1) * 4].transpose([1, 0, 2])),
                    wait=([(s_pe, wo_credit[fb % 2])] if i == 0 else []), inc=d_w, inc_k=16)
            return last

        rms_done = []
        prev_ssq = 0
        for t in range(NTT):
            E(self.AQ, lambda e, t=t: e.activation(
                sqvA[:, :, :], bview[:, 0:8, t, :], AF.Square),
                wait=[(s_ve, all_heads), (s_pe, prev_ssq), (s_po, self.last_rot_po)], inc=s_ac)
            pa = E(self.AQ, lambda e, t=t: e.activation(
                sqvB[:, :, :], bview[:, 8:16, t, :], AF.Square), inc=s_ac)
            for hh2 in range(NH):
                w = [(s_ac, pa)]
                if hh2 == 0:
                    w.append((s_ve, psB_ve))
                rhs = self.rsA[:, (hh2 % 8) * P:((hh2 % 8) + 1) * P] if hh2 < 8 else \
                    self.rsB[:, (hh2 % 8) * P:((hh2 % 8) + 1) * P]
                E(self.TQ, lambda e, hh2=hh2, rhs=rhs: e.matmul(
                    psB[0:1, 0:P], self.onescol[:, :], rhs,
                    start=(hh2 == 0), stop=(hh2 == NH - 1)),
                    wait=w, inc=(s_pe if hh2 == NH - 1 else None))
            ssq = s_pe.n
            prev_ssq = ssq
            E(self.VQ, lambda e: e.tensor_copy(self.sq[:, 0:P], psB[:, 0:P]),
              wait=[(s_pe, ssq)], inc=s_ve)
            pv = E(self.VQ, lambda e: e.tensor_scalar(self.srow[:, :], self.sq[0:1, 0:P],
                                                      1.0 / DM, None, ALU.mult),
                   wait=self.vf(), inc=s_ve)
            E(self.AQ, lambda e: e.activation(self.srow[:, :], self.srow[:, :], AF.Ln, bias=float(EPS)),
              wait=[(s_ve, pv)], inc=s_ac)
            pa = E(self.AQ, lambda e: e.activation(self.srow_bf[:, :], self.srow[:, :], AF.Exp, scale=-0.5),
                   inc=s_ac)
            w = [(s_ac, pa), (s_ve, pv)]
            if rms_done:
                w.append((s_ve, rms_done[-1]))
            E(self.TQ, lambda e: e.matmul(psB[:, 512:640], self.onesb[:, :], self.srow_bf[:, :],
                                          start=True, stop=True),
              wait=w, inc=s_pe)
            bc = s_pe.n
            for hh2 in range(NH):
                rms_done.append(E(self.VQ, lambda e, hh2=hh2, t=t: e.scalar_tensor_tensor(
                    bview[:, hh2, t, :], bview[:, hh2, t, :], self.c_rmsw[:, hh2:hh2 + 1],
                    psB[:, 512:640], ALU.mult, ALU.mult),
                    wait=([(s_pe, bc)] if hh2 == 0 else []), inc=s_ve))
            psB_ve = rms_done[-1]

        self.dump("rmsd", self.bB[:, :], wait=[(s_ve, rms_done[-1])])
        wo_ready = {0: load_wo(0), 1: load_wo(1)}
        xv = []
        for fb in range(4):
            wo_sl = wo_slot[fb % 2]
            for t in range(NTT):
                for hh2 in range(NH):
                    w = []
                    if hh2 == 0:
                        w = [(d_w, wo_ready[fb]), (s_ve, rms_done[-1])]
                        if len(xv) >= 2:
                            w.append((s_ve, xv[-2]))
                    E(self.TQ, lambda e, fb=fb, t=t, hh2=hh2, wo_sl=wo_sl: e.matmul(
                        psC[:, (t % 2) * 512:((t % 2) + 1) * 512],
                        self.bB[:, hh2 * 1024 + t * P: hh2 * 1024 + (t + 1) * P],
                        wo_sl[:, hh2 * 512:(hh2 + 1) * 512],
                        start=(hh2 == 0), stop=(hh2 == NH - 1)),
                        wait=w, inc=(s_pe if hh2 == NH - 1 else None))
                grp = s_pe.n
                xv.append(E(self.VQ, lambda e, fb=fb, t=t: e.tensor_tensor(
                    self.x_sb[:, t * DM + fb * 512: t * DM + (fb + 1) * 512],
                    self.x_sb[:, t * DM + fb * 512: t * DM + (fb + 1) * 512],
                    psC[:, (t % 2) * 512:((t % 2) + 1) * 512], ALU.add),
                    wait=[(s_pe, grp)] + self.X_DEP, inc=s_ve))
            wo_credit[fb % 2] = s_pe.n
            if fb + 2 < 4:
                wo_ready[fb + 2] = load_wo(fb + 2)
        self.X_DEP = [(s_ve, s_ve.n)]

    def ffn(self, keep_stats=False):
        E = self.emit
        s_pe, s_ve, s_ac = self.s_pe, self.s_ve, self.s_ac
        d_w = self.d_w
        psA, psB, psC, psT = self.psA, self.psB, self.psC, self.psT
        xlnT, bC, bB = self.xlnT, self.bC, self.bB

        self.ln(keep_stats=keep_stats)

        for tt in range(NTT):
            for kc in range(NKC):
                w = []
                if tt == 0 and kc == 0:
                    w = self.xlnT_dep + [(self.d_w, self.CONSTS[1]),
                                         (s_ve, self._psT_guard())]
                E(self.TQ, lambda e, tt=tt, kc=kc: e.matmul(
                    psT[:, tt * P: tt * P + 8],
                    xlnT[:, kc * 1024 + tt * P: kc * 1024 + (tt + 1) * P],
                    self.c_gate[:, kc * 8:(kc + 1) * 8],
                    start=(kc == 0), stop=(kc == NKC - 1)),
                    wait=w, inc=(s_pe if kc == NKC - 1 else None))
        gmm = s_pe.n
        for tt in range(NTT):
            gp = psT[:, tt * P: tt * P + 8]
            E(self.VQ, lambda e, gp=gp: e.tensor_copy(self.cwsc[:, :], gp),
              wait=[(s_pe, gmm)], inc=s_ve)
            pv0 = E(self.VQ, lambda e: e.reduce_max(self.sc[:, 8:9], self.cwsc[:, :], axis=AXX,
                                                    negate=True), wait=self.vf(), inc=s_ve)
            pa = E(self.AQ, lambda e: e.activation(self.cwsc[:, :], self.cwsc[:, :], AF.Exp,
                                                   bias=self.sc[:, 8:9]),
                   wait=[(s_ve, pv0)], inc=s_ac)
            E(self.VQ, lambda e: e.reduce_sum(self.sc[:, 9:10], self.cwsc[:, :], axis=AXX),
              wait=[(s_ac, pa)], inc=s_ve)
            E(self.VQ, lambda e: e.reciprocal(self.sc[:, 10:11], self.sc[:, 9:10]),
              wait=self.vf(), inc=s_ve)
            E(self.VQ, lambda e: e.tensor_scalar(self.cwsc[:, :], self.cwsc[:, :],
                                                 self.sc[:, 10:11], None, ALU.mult),
              wait=self.vf(), inc=s_ve)
            E(self.VQ, lambda e: e.reduce_max(self.sc[:, 11:12], self.cwsc[:, :], axis=AXX),
              wait=self.vf(), inc=s_ve)
            E(self.VQ, lambda e: e.tensor_single_scalar(self.cwm[:, :], self.cwsc[:, :],
                                                        self.sc[:, 11:12], ALU.is_lt),
              wait=self.vf(), inc=s_ve)
            E(self.VQ, lambda e: e.tensor_tensor(self.cwm[:, :], self.cwm[:, :], self.cwsc[:, :],
                                                 ALU.mult), wait=self.vf(), inc=s_ve)
            E(self.VQ, lambda e: e.reduce_max(self.sc[:, 12:13], self.cwm[:, :], axis=AXX),
              wait=self.vf(), inc=s_ve)
            E(self.VQ, lambda e: e.tensor_tensor(self.sc[:, 13:14], self.sc[:, 11:12],
                                                 self.sc[:, 12:13], ALU.add),
              wait=self.vf(), inc=s_ve)
            E(self.VQ, lambda e: e.reciprocal(self.sc[:, 13:14], self.sc[:, 13:14]),
              wait=self.vf(), inc=s_ve)
            E(self.VQ, lambda e: e.tensor_single_scalar(self.cwm[:, :], self.cwsc[:, :],
                                                        self.sc[:, 12:13], ALU.is_ge),
              wait=self.vf(), inc=s_ve)
            E(self.VQ, lambda e: e.scalar_tensor_tensor(
                self.cwsc[:, :], self.cwsc[:, :], self.sc[:, 13:14], self.cwm[:, :],
                ALU.mult, ALU.mult), wait=self.vf(), inc=s_ve)
            E(self.VQ, lambda e, tt=tt: e.tensor_copy(self.cw_sb[:, tt * 8:(tt + 1) * 8],
                                                      self.cwsc[:, :]), wait=self.vf(), inc=s_ve)
        gate_done = s_ve.n
        self.psT_ev[0] = gate_done
        self.psT_ev[1] = gate_done
        self.dump("cw", self.cw_sb[:, :], wait=[(s_ve, gate_done)])
        self.dump("xlnT2", xlnT[:, :], wait=self.xlnT_dep)

        rgu = [bC[:, 0:4096], bC[:, 4096:8192]]
        rd_ = [bC[:, 8192:16384], bC[:, 16384:24576]]
        rgu_credit = [s_pe.n, s_pe.n]
        rd_credit = [s_pe.n, s_pe.n]
        psA_free = 0
        psB_free = 0
        psC_free = [0, 0]
        hmul = []
        dn_idx = 0

        def load_gu(e_, ic, slot):
            if e_ < 8:
                srcg = self.wg_d[e_, ic]
                srcu = self.wu_d[e_, ic]
            else:
                srcg = self.sg_d[ic]
                srcu = self.su_d[ic]
            E(self.GQ, lambda e, slot=slot, srcg=srcg: e.dma_start(
                out=rgu[slot][:, 0:2048], in_=srcg),
                wait=[(s_pe, rgu_credit[slot])], inc=d_w, inc_k=16)
            return E(self.GQ, lambda e, slot=slot, srcu=srcu: e.dma_start(
                out=rgu[slot][:, 2048:4096], in_=srcu), inc=d_w, inc_k=16)

        def load_wd(e_, fb, slot):
            nkc_d = 8 if e_ < 8 else NKC
            sl = rd_[slot].rearrange("p (k j) -> p k j", j=512)
            last = None
            for i in range(nkc_d // 4):
                src = self.wd_d[e_, fb, i * 4:(i + 1) * 4] if e_ < 8 else self.sd_d[fb, i * 4:(i + 1) * 4]
                last = E(self.GQ, lambda e, sl=sl, src=src, i=i: e.dma_start(
                    out=sl[:, i * 4:(i + 1) * 4, :], in_=src.transpose([1, 0, 2])),
                    wait=([(s_pe, rd_credit[slot])] if i == 0 else []), inc=d_w, inc_k=16)
            return last

        gu_ready = {}
        slot_of = {}
        seq = [(e_, ic) for e_ in range(9) for ic in range(8 if e_ < 8 else NKC)]
        for i, (e_, ic) in enumerate(seq[:2]):
            slot_of[(e_, ic)] = i % 2
            gu_ready[(e_, ic)] = load_gu(e_, ic, i % 2)

        wd_ready = {}
        wd_slot_of = {}
        dn_seq = [(e_, fb) for e_ in range(9) for fb in range(4)]

        hprev = [0] * 16
        down_done = {}
        hwrite = {}
        xv = [0, 0]

        for i, (e_, ic) in enumerate(seq):
            slot = slot_of[(e_, ic)]
            if e_ < 8:
                hsl = bB[:, (e_ % 2) * 8192 + ic * 1024: (e_ % 2) * 8192 + (ic + 1) * 1024]
            else:
                hsl = bB[:, ic * 1024:(ic + 1) * 1024]
            for blk in range(2):
                for kc in range(NKC):
                    w = []
                    if blk == 0 and kc == 0:
                        w = [(d_w, gu_ready[(e_, ic)]), (s_ac, psA_free)]
                        if e_ == 8 and ic == 0:
                            w.append((s_ve, self._ffn_shared_guard))
                    E(self.TQ, lambda e, blk=blk, kc=kc, slot=slot: e.matmul(
                        psA[:, blk * 512:(blk + 1) * 512],
                        rgu[slot][:, kc * P:(kc + 1) * P],
                        xlnT[:, kc * 1024 + blk * 512: kc * 1024 + (blk + 1) * 512],
                        start=(kc == 0), stop=(kc == NKC - 1)),
                        wait=w, inc=(s_pe if kc == NKC - 1 else None))
            gdone = s_pe.n
            for blk in range(2):
                for kc in range(NKC):
                    w = []
                    if blk == 0 and kc == 0:
                        w = [(s_ve, psB_free)]
                    E(self.TQ, lambda e, blk=blk, kc=kc, slot=slot: e.matmul(
                        psB[:, blk * 512:(blk + 1) * 512],
                        rgu[slot][:, 2048 + kc * P: 2048 + (kc + 1) * P],
                        xlnT[:, kc * 1024 + blk * 512: kc * 1024 + (blk + 1) * 512],
                        start=(kc == 0), stop=(kc == NKC - 1)),
                        wait=w, inc=(s_pe if kc == NKC - 1 else None))
            udone = s_pe.n
            pa = E(self.AQ, lambda e: e.activation(self.sq[:, 0:1024], psA[:, :], AF.Silu),
                   wait=[(s_pe, gdone), (s_ve, psB_free)], inc=s_ac)
            psA_free = pa
            rgu_credit[slot] = udone
            if i + 2 < len(seq):
                slot_of[seq[i + 2]] = (i + 2) % 2
                gu_ready[seq[i + 2]] = load_gu(seq[i + 2][0], seq[i + 2][1], (i + 2) % 2)
            hw = E(self.VQ, lambda e, hsl=hsl: e.tensor_tensor(hsl, self.sq[:, 0:1024], psB[:, :],
                                                               ALU.mult),
                   wait=[(s_pe, udone), (s_ac, pa)], inc=s_ve)
            psB_free = hw
            hwrite[(e_, ic)] = hw
            if e_ == 0 and ic == 7 and not getattr(self, '_h0_dumped', False):
                self._h0_dumped = True
                self.dump("h0", self.bB[:, 0:8192], wait=[(s_ve, hw)])

            nic = 8 if e_ < 8 else NKC
            if ic == nic - 1:
                for fb in range(2):
                    j = dn_idx % 2
                    wd_ready[(e_, fb)] = load_wd(e_, fb, j)
                    wd_slot_of[(e_, fb)] = j
                    dn_idx += 1
                hdep = hwrite[(e_, nic - 1)]
                for fb in range(4):
                    j = wd_slot_of[(e_, fb)]
                    for tt in range(NTT):
                        pslot = (fb * NTT + tt) % 2
                        for kc2 in range(nic):
                            w = []
                            if kc2 == 0:
                                w = [(d_w, wd_ready[(e_, fb)]), (s_ve, hdep),
                                     (s_ve, psC_free[pslot])]
                            if e_ < 8:
                                hap = bB[:, (e_ % 2) * 8192 + kc2 * 1024 + tt * P:
                                         (e_ % 2) * 8192 + kc2 * 1024 + (tt + 1) * P]
                            else:
                                hap = bB[:, kc2 * 1024 + tt * P: kc2 * 1024 + (tt + 1) * P]
                            E(self.TQ, lambda e, hap=hap, j=j, kc2=kc2, pslot=pslot, nic=nic: e.matmul(
                                psC[:, pslot * 512:(pslot + 1) * 512],
                                hap, rd_[j][:, kc2 * 512:(kc2 + 1) * 512],
                                start=(kc2 == 0), stop=(kc2 == nic - 1)),
                                wait=w, inc=(s_pe if kc2 == nic - 1 else None))
                        grp = s_pe.n
                        xsl = self.x_sb[:, tt * DM + fb * 512: tt * DM + (fb + 1) * 512]
                        scal = (1.0 if e_ == 8
                                else self.cw_sb[:, tt * 8 + e_: tt * 8 + e_ + 1])
                        w2 = [(s_pe, grp)]
                        if e_ == 0 and tt == 0 and fb == 0:
                            w2 += self.X_DEP + [(s_ve, gate_done)]
                        psC_free[pslot] = E(self.VQ, lambda e, xsl=xsl, scal=scal, pslot=pslot:
                                            e.scalar_tensor_tensor(
                                                xsl, psC[:, pslot * 512:(pslot + 1) * 512], scal,
                                                xsl, ALU.mult, ALU.add),
                                            wait=w2, inc=s_ve)
                    rd_credit[j] = s_pe.n
                    if fb + 2 < 4:
                        j2 = dn_idx % 2
                        wd_ready[(e_, fb + 2)] = load_wd(e_, fb + 2, j2)
                        wd_slot_of[(e_, fb + 2)] = j2
                        dn_idx += 1
                if e_ == 7:
                    self._ffn_shared_guard = s_ve.n
        self.X_DEP = [(s_ve, s_ve.n)]

    def attn_b(self):
        E = self.emit
        s_pe, s_ve, s_ac, s_po = self.s_pe, self.s_ve, self.s_ac, self.s_po
        d_w, d_kv, d_sc = self.d_w, self.d_kv, self.d_kb
        psA, psB, psC, psT = self.psA, self.psB, self.psC, self.psT
        xlnT, bC, bB = self.xlnT, self.bC, self.bB

        self.ln(keep_stats=True)
        ln_stats = (self.s_ve, self.s_ve.n)

        for blk in range(2):
            w = [(self.d_w, self.CONSTS[1])]
            if blk == 0:
                w.append((self.s_ve, self._psT_guard()))
            E(self.TQ, lambda e, blk=blk: e.matmul(
                psT[:, blk * 512:(blk + 1) * 512], self.onesb[:, :],
                self.bvrow_bf[0:1, blk * 512:(blk + 1) * 512], start=True, stop=True),
                wait=w, inc=(s_pe if blk == 1 else None))
        bv2 = s_pe.n
        cp1 = E(self.VQ, lambda e: e.tensor_copy(self.bv_bc[:, 0:1024], psT[:, :]),
                wait=[(s_pe, bv2)], inc=s_ve)
        for blk in range(2):
            E(self.TQ, lambda e, blk=blk: e.matmul(
                psT[:, blk * 512:(blk + 1) * 512], self.onesb[:, :],
                self.bvrow_bf[0:1, 1024 + blk * 512: 1024 + (blk + 1) * 512],
                start=True, stop=True),
                wait=([(s_ve, cp1)] if blk == 0 else []), inc=(s_pe if blk == 1 else None))
        bv4 = s_pe.n
        pv_bv = E(self.VQ, lambda e: e.tensor_copy(self.bv_bc[:, 1024:2048], psT[:, :]),
                  wait=[(s_pe, bv4)], inc=s_ve)
        self.psT_ev[0] = pv_bv
        self.psT_ev[1] = pv_bv

        xres = []
        for t in range(NTT):
            xres.append(E(self.VQ, lambda e, t=t: e.tensor_scalar(
                self.x_sb[:, t * DM:(t + 1) * DM], self.x_sb[:, t * DM:(t + 1) * DM],
                self.ln3a[:, t:t + 1], self.ln3b[:, t:t + 1], ALU.mult, ALU.add),
                wait=[ln_stats] + self.X_DEP, inc=s_ve))
        self.X_DEP = [(s_ve, xres[-1])]

        wk_slot = [bC[:, 0:2048], bC[:, 2048:4096]]
        kbuf = bC[:, 4096:5120]
        vbuf = [bC[:, 5120:5632], bC[:, 5632:6144]]
        wv_slot = [bC[:, 8192:16384], bC[:, 16384:24576]]
        wk_credit = [s_pe.n, s_pe.n]
        wv_credit = [s_pe.n, s_pe.n]
        psA_ac = 0
        psB_free = [0, 0]

        def load_wkb(h):
            return E(self.GQ, lambda e, h=h: e.dma_start(out=wk_slot[h % 2], in_=self.wkb[h]),
                     wait=[(s_pe, wk_credit[h % 2])], inc=d_w, inc_k=16)

        def load_wvb(hg):
            sl = wv_slot[hg % 2].rearrange("p (k j) -> p k j", j=512)
            last = None
            for i in range(4):
                last = E(self.GQ, lambda e, hg=hg, i=i, sl=sl: e.dma_start(
                    out=sl[:, i * 4:(i + 1) * 4, :],
                    in_=self.wvb[hg, i * 4:(i + 1) * 4].transpose([1, 0, 2])),
                    wait=([(s_pe, wv_credit[hg % 2])] if i == 0 else []), inc=d_w, inc_k=16)
            return last

        wk_ready = {0: load_wkb(0)}
        kstore = 0
        for h in range(NH):
            if h + 1 < NH:
                wk_ready[h + 1] = load_wkb(h + 1)
            for blk in range(2):
                for kc in range(NKC):
                    w = []
                    if blk == 0 and kc == 0:
                        w = [(d_w, wk_ready[h]), (s_ac, psA_ac)] + self.xlnT_dep
                    E(self.TQ, lambda e, blk=blk, kc=kc, h=h: e.matmul(
                        psA[:, blk * 512:(blk + 1) * 512],
                        wk_slot[h % 2][:, kc * P:(kc + 1) * P],
                        xlnT[:, kc * 1024 + blk * 512: kc * 1024 + (blk + 1) * 512],
                        start=(kc == 0), stop=(kc == NKC - 1)),
                        wait=w, inc=(s_pe if kc == NKC - 1 else None))
            kproj = s_pe.n
            wk_credit[h % 2] = kproj
            psA_ac = E(self.AQ, lambda e, h=h: e.activation(kbuf, psA[:, :], AF.Identity,
                                                            bias=self.c_bk[:, h:h + 1]),
                       wait=[(s_pe, kproj), (d_sc, kstore)], inc=s_ac)
            E(self.AQ, lambda e, h=h: e.dma_start(out=self.agk_in[h * P:(h + 1) * P, :], in_=kbuf),
              inc=d_sc, inc_k=16)
            kstore = d_sc.n
        E(self.GQ, lambda e: e.collective_compute(
            "AllGather", ALU.bypass, replica_groups=self.RG,
            ins=[self.agk_in.ap().opt()], outs=[self.agk_out.ap().opt()]),
            wait=[(d_sc, kstore)], inc=self.cck)

        wv_ready = {0: load_wvb(0)}
        vstore = 0
        vprev = [0, 0]
        vdma = [0, 0]
        for hg in range(4):
            if hg + 1 < 4:
                wv_ready[hg + 1] = load_wvb(hg + 1)
            for tt in range(NTT):
                pslot = tt % 2
                for kc in range(NKC):
                    w = []
                    if tt == 0 and kc == 0:
                        w = [(d_w, wv_ready[hg]), (s_ve, psB_free[pslot])] + self.xlnT_dep
                    elif kc == 0:
                        w = [(s_ve, vprev[pslot])]
                    E(self.TQ, lambda e, tt=tt, kc=kc, hg=hg, pslot=pslot: e.matmul(
                        psB[:, pslot * 512:(pslot + 1) * 512],
                        xlnT[:, kc * 1024 + tt * P: kc * 1024 + (tt + 1) * P],
                        wv_slot[hg % 2][:, kc * 512:(kc + 1) * 512],
                        start=(kc == 0), stop=(kc == NKC - 1)),
                        wait=w, inc=(s_pe if kc == NKC - 1 else None))
                vg = s_pe.n
                vprev[pslot] = E(self.VQ, lambda e, hg=hg, tt=tt, pslot=pslot: e.scalar_tensor_tensor(
                    vbuf[pslot], psB[:, pslot * 512:(pslot + 1) * 512], 1.0,
                    self.bv_bc[:, hg * 512:(hg + 1) * 512], ALU.mult, ALU.add),
                    wait=[(s_pe, vg), (s_ve, pv_bv), (d_sc, vdma[pslot])], inc=s_ve)
                E(self.AQ, lambda e, hg=hg, tt=tt, pslot=pslot: e.dma_start(
                    out=self.agv_in[tt * P:(tt + 1) * P, hg * 512:(hg + 1) * 512], in_=vbuf[pslot]),
                    wait=[(s_ve, vprev[pslot])], inc=d_sc, inc_k=16)
                vdma[pslot] = d_sc.n
                psB_free[pslot] = vprev[pslot]
            wv_credit[hg % 2] = s_pe.n
        vstore = d_sc.n
        E(self.GQ, lambda e: e.collective_compute(
            "AllGather", ALU.bypass, replica_groups=self.RG,
            ins=[self.agv_in.ap().opt()], outs=[self.agv_out.ap().opt()]),
            wait=[(d_sc, vstore)], inc=self.ccv)

        wq_slot = [bC[:, 0:2048], bC[:, 2048:4096]]
        wq_credit = [wk_credit[0], wk_credit[1]]
        wq_ready = {}

        def load_wqb(h):
            return E(self.GQ, lambda e, h=h: e.dma_start(out=wq_slot[h % 2], in_=self.wqb[h]),
                     wait=[(s_pe, wq_credit[h % 2])], inc=d_w, inc_k=16)

        wq_ready[0] = load_wqb(0)
        pa_prev = psA_ac
        for h in range(NH):
            if h + 1 < NH:
                wq_ready[h + 1] = load_wqb(h + 1)
            for blk in range(2):
                for kc in range(NKC):
                    w = []
                    if blk == 0 and kc == 0:
                        w = [(d_w, wq_ready[h]), (s_ac, pa_prev)]
                    E(self.TQ, lambda e, blk=blk, kc=kc, h=h: e.matmul(
                        psA[:, blk * 512:(blk + 1) * 512],
                        wq_slot[h % 2][:, kc * P:(kc + 1) * P],
                        xlnT[:, kc * 1024 + blk * 512: kc * 1024 + (blk + 1) * 512],
                        start=(kc == 0), stop=(kc == NKC - 1)),
                        wait=w, inc=(s_pe if kc == NKC - 1 else None))
            qproj = s_pe.n
            wq_credit[h % 2] = qproj
            pa_prev = E(self.AQ, lambda e, h=h: e.activation(
                bB[:, h * 1024:(h + 1) * 1024], psA[:, :], AF.Identity, bias=self.c_bq[:, h:h + 1]),
                wait=[(s_pe, qproj)], inc=s_ac)
        qall = pa_prev

        kh = [bC[:, 4096:8192], bC[:, 8192:12288]]
        vh = [bC[:, 12288:16512], bC[:, 16512:20736]]
        ptb = [bC[:, 20736:21760], bC[:, 21760:22784]]
        ms0 = E(self.VQ, lambda e: e.memset(vh[0], 1.0), wait=[(s_ve, psB_free[0]),
                                                              (s_ve, psB_free[1])], inc=s_ve)
        ms1 = E(self.VQ, lambda e: e.memset(vh[1], 1.0), inc=s_ve)
        kv_credit = [ms0, ms1]
        kv_credit_pe = [0, 0]
        kv_ready = {}

        def load_kv(h):
            sl = h % 2
            w = [(self.cck, 1), (self.ccv, 1), (s_pe, kv_credit_pe[sl]), (s_ve, kv_credit[sl])]
            for r in range(4):
                E(self.YQ, lambda e, h=h, r=r, sl=sl: e.dma_start(
                    out=kh[sl][:, r * 1024:(r + 1) * 1024],
                    in_=self.agk_out[r * DM + h * P: r * DM + (h + 1) * P, :]),
                    wait=(w if r == 0 else []), inc=d_kv, inc_k=16)
            vv = self.agv_out.ap().rearrange("(s p) d -> p s d", p=P)
            vhv = vh[sl].rearrange("p (s d) -> p s d", d=132)
            for i in range(4):
                E(self.YQ, lambda e, h=h, i=i, sl=sl, vv=vv, vhv=vhv: e.dma_start(
                    out=vhv[:, i * 8:(i + 1) * 8, 0:P],
                    in_=vv[:, i * 8:(i + 1) * 8, h * P:(h + 1) * P]),
                    inc=d_kv, inc_k=16)
            return d_kv.n

        kv_ready[0] = load_kv(0)
        pv_slots = [psC[:, 0:132], psC[:, 132:264], psC[:, 264:396],
                    psC[:, 512:644], psC[:, 644:776], psC[:, 776:908],
                    psT[:, 0:132], psT[:, 132:264]]
        sT_ps = [psA, psB]
        exp_done = [0, 0]
        pv_evac = self._psT_guard()
        psT_w = self.psT_ev
        for h in range(NH):
            sl = h % 2
            if h + 1 < NH:
                kv_ready[h + 1] = load_kv(h + 1)
            for sc_ in range(32):
                ps = sT_ps[sc_ % 2]
                for blk in range(2):
                    w = []
                    if blk == 0:
                        w = [(s_ac, exp_done[sc_ % 2])]
                        if sc_ == 0:
                            w += [(d_kv, kv_ready[h]), (s_ac, qall)]
                            if h == 0:
                                w += [(s_ve, pv_bv)]
                        if h == 0 and sc_ == 1:
                            w += [(s_ve, psB_free[0]), (s_ve, psB_free[1])]
                    E(self.TQ, lambda e, sc_=sc_, blk=blk, h=h, sl=sl, ps=ps: e.matmul(
                        ps[:, blk * 512:(blk + 1) * 512],
                        kh[sl][:, sc_ * P:(sc_ + 1) * P],
                        bB[:, h * 1024 + blk * 512: h * 1024 + (blk + 1) * 512],
                        start=True, stop=True),
                        wait=w, inc=(s_pe if blk == 1 else None))
                sTv = s_pe.n
                exp_done[sc_ % 2] = E(self.AQ, lambda e, sc_=sc_, ps=ps: e.activation(
                    ptb[sc_ % 2], ps[:, :], AF.Exp, scale=RSQ, bias=-MSHIFT),
                    wait=[(s_pe, sTv)], inc=s_ac)
                for qt in range(NTT):
                    w = [(s_ac, exp_done[sc_ % 2])] if qt == 0 else []
                    if sc_ == 0 and qt == 0:
                        w.append((s_ve, pv_evac))
                    st_ = (sc_ == 0 and qt in (0, 3, 6))
                    sp_ = (sc_ == 31 and qt in (2, 5, 7))
                    E(self.TQ, lambda e, sc_=sc_, qt=qt, sl=sl, st_=st_, sp_=sp_: e.matmul(
                        pv_slots[qt], ptb[sc_ % 2][:, qt * P:(qt + 1) * P],
                        vh[sl][:, sc_ * 132: sc_ * 132 + 132],
                        start=st_, stop=sp_),
                        wait=w, inc=(s_pe if (sc_ == 31 and qt == NTT - 1) else None))
            pvdone = s_pe.n
            kv_credit_pe[sl] = pvdone
            for qt in range(NTT):
                E(self.VQ, lambda e, qt=qt: e.reciprocal(self.sc[:, 8 + qt:9 + qt],
                                                         pv_slots[qt][:, 128:129]),
                  wait=([(s_pe, pvdone)] if qt == 0 else []), inc=s_ve)
            for qt in range(NTT):
                E(self.VQ, lambda e, qt=qt, h=h: e.scalar_tensor_tensor(
                    self.x_sb[:, qt * DM + h * P: qt * DM + (h + 1) * P],
                    pv_slots[qt][:, 0:P], self.sc[:, 8 + qt:9 + qt],
                    self.x_sb[:, qt * DM + h * P: qt * DM + (h + 1) * P],
                    ALU.mult, ALU.add), wait=(self.vf() if qt == 0 else []), inc=s_ve)
            pv_evac = self.s_ve.n
            kv_credit[sl] = pv_evac
        self.psT_ev[0] = pv_evac
        self.psT_ev[1] = pv_evac
        self.X_DEP = [(s_ve, pv_evac)]

    def _psT_guard(self):
        return max(self.psT_ev)

    def store(self):
        E = self.emit
        E(self.YQ, lambda e: e.dma_start(out=self.x_out[:, :], in_=self.x_sb[:, :]),
          wait=self.X_DEP + [(self.s_ac, self.s_ac.n), (self.s_po, self.s_po.n)],
          inc=self.d_st, inc_k=16)
        fin = self.d_st.n

        def g(eng, fin=fin):
            eng.wait_ge(self.d_st.h, fin)
        self.YQ.fs.append(g)

    def build(self):
        self.declare()
        self._ffn_shared_guard = 0
        self.initial_loads()
        if self.upto >= 1:
            self.attn_a()
        if self.upto >= 2:
            self.ffn(keep_stats=False)
        if self.upto >= 3:
            self.attn_b()
        if self.upto >= 4:
            self.ffn(keep_stats=False)
        self.store()
        with self.nc.Block() as block:
            block.tensor(lambda eng: [f(eng) for f in self.TQ.fs])
            block.vector(lambda eng: [f(eng) for f in self.VQ.fs])
            block.scalar(lambda eng: [f(eng) for f in self.AQ.fs])
            block.gpsimd(lambda eng: [f(eng) for f in self.GQ.fs])
            block.sync(lambda eng: [f(eng) for f in self.YQ.fs])
        return self.nc


def build_program(upto=4, dbg=(), aa_heads=NH, use_cc=True, ln_only=False, aa_sub=9, no_rot=False):
    nc = bass.Bass("TRN2", target_bir_lowering=False, detect_race_conditions=False)
    ctx = ExitStack()
    with ctx:
        mk = MK(nc, ctx, upto, dbg=dbg, aa_heads=aa_heads, use_cc=use_cc, ln_only=ln_only, aa_sub=aa_sub, no_rot=no_rot)
        mk.build()
    return nc



def _tile_lhsT_per_head(W):
    w4 = W.reshape(NKC, P, NH, P)
    return np.ascontiguousarray(w4.transpose(2, 1, 0, 3).reshape(NH, P, DM))


def _tile_rhs_fb(W, fbw=512):
    n = W.shape[1]
    w4 = W.reshape(NKC, P, n // fbw, fbw)
    return np.ascontiguousarray(w4.transpose(2, 0, 1, 3))


def _rotary_tables(pos0):
    D4 = HD // 4
    freq = (1.0 / 1024.0) ** np.linspace(0.0, 1.0, D4)
    t = np.arange(pos0, pos0 + 1024, dtype=np.float64)
    theta = t[:, None] * freq[None, :]
    cos = np.cos(theta)
    sin = np.sin(theta)
    cos_c = np.ascontiguousarray(cos.T, dtype=np.float32)
    sin_c = np.ascontiguousarray(sin.T, dtype=np.float32)
    cos_t = cos.reshape(NTT, P, 32).transpose(1, 0, 2).reshape(P, 256)
    sin_t = sin.reshape(NTT, P, 32).transpose(1, 0, 2).reshape(P, 256)
    return (cos_c.astype(ml_dtypes.bfloat16), sin_c.astype(ml_dtypes.bfloat16),
            np.ascontiguousarray(cos_t).astype(ml_dtypes.bfloat16),
            np.ascontiguousarray(sin_t).astype(ml_dtypes.bfloat16))


def _lightning_tables():
    i = np.arange(P, dtype=np.float64)
    dmaskt = np.zeros((P, DM))
    qdect = np.zeros((P, DM))
    kdecc = np.zeros((P, NH))
    for h in range(NH):
        r = _RH[h]
        diff = i[None, :] - i[:, None]
        dmaskt[:, h * P:(h + 1) * P] = np.where(diff >= 0, r ** np.maximum(diff, 0.0), 0.0)
        qdect[:, h * P:(h + 1) * P] = np.broadcast_to(r ** (i + 1.0), (P, P))
        kdecc[:, h] = r ** (127.0 - i)
    return (dmaskt.astype(ml_dtypes.bfloat16), qdect.astype(ml_dtypes.bfloat16),
            kdecc.astype(np.float32))


def _dw_cols(gpos):
    out = np.zeros((P, 64))
    for j in range(4):
        if j < gpos:
            out[:, j * NH:(j + 1) * NH] = (_RC ** (8.0 * (gpos - 1 - j)))[None, :]
    return out.astype(np.float32)


_CACHE = {}


def kernel(L0_attn_a_x, L0_attn_a_Wq, L0_attn_a_Wk, L0_attn_a_Wv, L0_attn_a_Wo, L0_attn_a_rms_w,
           L0_attn_b_Wq, L0_attn_b_bq, L0_attn_b_Wk, L0_attn_b_bk, L0_attn_b_Wv, L0_attn_b_bv,
           L0_ffn_gate_w, L0_ffn_wg, L0_ffn_wu, L0_ffn_wd, L0_ffn_sg, L0_ffn_su, L0_ffn_sd,
           _upto=4):
    f32 = np.float32
    x = np.asarray(L0_attn_a_x, f32)

    shared = {
        "wqa": _tile_lhsT_per_head(np.asarray(L0_attn_a_Wq, f32)),
        "wka": _tile_lhsT_per_head(np.asarray(L0_attn_a_Wk, f32)),
        "wva": _tile_rhs_fb(np.asarray(L0_attn_a_Wv, f32)),
        "woa": _tile_rhs_fb(np.asarray(L0_attn_a_Wo, f32)),
        "rmsw": np.ascontiguousarray(np.asarray(L0_attn_a_rms_w, f32).reshape(NH, P).T),
        "gatew": np.ascontiguousarray(
            np.asarray(L0_ffn_gate_w, f32).T.reshape(NKC, P, 8).transpose(1, 0, 2).reshape(P, P)),
        "wg": np.ascontiguousarray(np.asarray(L0_ffn_wg, f32).reshape(8, NKC, P, 8, P)
                                   .transpose(0, 3, 2, 1, 4).reshape(8, 8, P, DM)),
        "wu": np.ascontiguousarray(np.asarray(L0_ffn_wu, f32).reshape(8, NKC, P, 8, P)
                                   .transpose(0, 3, 2, 1, 4).reshape(8, 8, P, DM)),
        "wd": np.ascontiguousarray(np.asarray(L0_ffn_wd, f32).reshape(8, 8, P, 4, 512)
                                   .transpose(0, 3, 1, 2, 4)),
        "sg": np.ascontiguousarray(np.asarray(L0_ffn_sg, f32).reshape(NKC, P, NKC, P)
                                   .transpose(2, 1, 0, 3).reshape(NKC, P, DM)),
        "su": np.ascontiguousarray(np.asarray(L0_ffn_su, f32).reshape(NKC, P, NKC, P)
                                   .transpose(2, 1, 0, 3).reshape(NKC, P, DM)),
        "sd": _tile_rhs_fb(np.asarray(L0_ffn_sd, f32)),
        "wqb": _tile_lhsT_per_head(np.asarray(L0_attn_b_Wq, f32)),
        "wkb": _tile_lhsT_per_head(np.asarray(L0_attn_b_Wk, f32)),
        "wvb": _tile_rhs_fb(np.asarray(L0_attn_b_Wv, f32)),
        "bqb": np.ascontiguousarray(np.asarray(L0_attn_b_bq, f32).reshape(NH, P).T),
        "bkb": np.ascontiguousarray(np.asarray(L0_attn_b_bk, f32).reshape(NH, P).T),
        "bvb": np.asarray(L0_attn_b_bv, f32).reshape(1, DM),
        "identf": np.eye(P, dtype=f32),
    }
    dmaskt, qdect, kdecc = _lightning_tables()
    shared["dmaskt"] = dmaskt
    shared["qdect"] = qdect
    shared["kdecc"] = kdecc

    in_maps = []
    for c in range(8):
        b = c // 4
        g = c % 4
        xs = x[b, g * 1024:(g + 1) * 1024]
        x_host = np.ascontiguousarray(xs.reshape(NTT, P, DM).transpose(1, 0, 2).reshape(P, NTT * DM))
        cos_c, sin_c, cos_t, sin_t = _rotary_tables(g * 1024)
        im = dict(shared)
        im["x_in"] = x_host
        im["cos_c"] = cos_c
        im["sin_c"] = sin_c
        im["cos_t"] = cos_t
        im["sin_t"] = sin_t
        im["dwcol"] = _dw_cols(g)
        im["constf"] = np.tile(np.array([[EPS, -0.5, RSQ, -MSHIFT, 0, 0, 0, 0]], np.float32), (P, 1))
        in_maps.append(im)

    key = _upto
    if key not in _CACHE:
        _CACHE[key] = build_program(upto=_upto)
    nc = _CACHE[key]
    res = run_bass_kernel_spmd(nc, in_maps, core_ids=list(range(8)))

    out = np.empty((2, 4096, DM), f32)
    for c in range(8):
        b = c // 4
        g = c % 4
        xo = np.asarray(res.results[c]["x_out"])
        out[b, g * 1024:(g + 1) * 1024] = (
            xo.reshape(P, NTT, DM).transpose(1, 0, 2).reshape(1024, DM))
    return out
